# version 20
# baseline (speedup 1.0000x reference)
import numpy as np
import ml_dtypes
from contextlib import ExitStack

import bass_rust
import concourse.bass as bass
import concourse.bacc as bacc
import concourse.mybir as mybir
from concourse.bass_utils import run_bass_kernel_spmd

F32 = mybir.dt.float32
BF16 = mybir.dt.bfloat16
I32 = mybir.dt.int32
AF = mybir.ActivationFunctionType
ALU = mybir.AluOpType
AX = mybir.AxisListType

T_TOK = 4096
D = 1024
NT = 32
NP = 8
NB = 4
NA = 32
NC = 4
HC = 24
DC = 8
HID = 3072
S127 = float(np.float32(1.0 / 127.0))
EPS = 1e-5


def build_nc(n_pairs=NP, do_ffn=True, debug=False, ffn_stage=3):
    if not do_ffn:
        ffn_stage = 0
    do_ffn = ffn_stage > 0
    nc = bacc.Bacc("TRN2", target_bir_lowering=False)
    x_in = nc.dram_tensor("x", [T_TOK, D], F32, kind="ExternalInput").ap()
    wgT = nc.dram_tensor("wgT", [D, HID], I32, kind="ExternalInput").ap()
    wuT = nc.dram_tensor("wuT", [D, HID], I32, kind="ExternalInput").ap()
    wdT = nc.dram_tensor("wdT", [HID, D], I32, kind="ExternalInput").ap()
    ident_in = nc.dram_tensor("ident", [128, 128], BF16, kind="ExternalInput").ap()
    y_out = nc.dram_tensor("y", [T_TOK, D], F32, kind="ExternalOutput").ap()
    attn_kind = "ExternalOutput" if debug else "Internal"
    attn_s = nc.dram_tensor("attn_s", [T_TOK, D], BF16, kind=attn_kind).ap()
    if debug:
        dbg_xcf = nc.dram_tensor("dbg_xcf", [128, T_TOK], BF16, kind="ExternalOutput").ap()
        dbg_mu = nc.dram_tensor("dbg_mu", [1, D], BF16, kind="ExternalOutput").ap()
        dbg_n2 = nc.dram_tensor("dbg_n2", [128, 64], F32, kind="ExternalOutput").ap()

    ctx = ExitStack()
    sb = lambda name, shape, dt: ctx.enter_context(nc.sbuf_tensor(name, shape, dt))
    ps = lambda name, shape, dt: ctx.enter_context(nc.psum_tensor(name, shape, dt))
    sem = lambda name: ctx.enter_context(nc.semaphore(name))

    with ctx:
        xcf = [sb(f"xcf{d}", [128, T_TOK], BF16) for d in range(DC)]
        xt = [sb(f"xt{i}", [128, D], F32) for i in range(2)]
        xnb = sb("xnb", [128, D], BF16)
        trash = sb("trash", [128, D], BF16)
        ln_sc = sb("ln_sc", [128, 10], F32)
        epsv = sb("epsv", [128, 1], F32)
        c30n = sb("c30n", [128, 1], F32)
        c30p = sb("c30p", [128, 1], F32)
        ident = sb("identS", [128, 128], BF16)
        ones1 = sb("ones1", [1, 128], BF16)
        onesc = sb("onesc", [128, 1], BF16)
        mu_rowb = sb("mu_rowb", [1, D], BF16)
        muT = sb("muT", [128, DC], F32)
        muTb = sb("muTb", [128, DC], BF16)
        mu_tile = sb("mu_tile", [128, D], BF16)
        vt = sb("vt", [128, T_TOK], BF16)
        outT = sb("outT", [128, T_TOK], BF16)
        e_h = [sb(f"e_h{k}", [128, 2048], BF16) for k in range(2)]
        n2 = sb("n2", [128, 64], F32)
        bbias = sb("bbias", [128, 64], F32)
        ebv = sb("ebv", [128, 64], F32)
        rslot = sb("rslot", [128, 64], F32)
        rsum = sb("rsum", [128, 64], F32)
        rinv = sb("rinv", [128, 64], F32)
        sval = sb("sval", [128, 64], F32)
        fin_buf = [sb(f"fin{k}", [128, 1024], BF16) for k in range(2)]
        wg_sl = [sb(f"wg_sl{k}", [128, 1024], BF16) for k in range(2)]
        wu_sl = [sb(f"wu_sl{k}", [128, 1024], BF16) for k in range(2)]
        wd_cs = [sb(f"wd_cs{k}", [128, HC * 128], BF16) for k in range(2)]
        up_b = sb("up_b", [128, 1024], BF16)
        hs = sb("hs", [128, HC * 1024], BF16)
        cgs = sb("cgs", [128, HC], F32)
        cus = sb("cus", [128, HC], F32)
        dn_bf = [sb(f"dn_bf{k}", [128, 512], BF16) for k in range(2)]
        asm = sb("asm", [128, 4 * D], BF16)
        at_re = [sb(f"at_re{k}", [128, D], BF16) for k in range(2)]
        y_buf = [sb(f"y_buf{k}", [128, D], F32) for k in range(2)]

        pA = ps("pA", [128, 1024], F32)
        pB = ps("pB", [128, 2048], F32)
        pC = ps("pC", [128, 1024], F32)
        pT = pA.bitcast(BF16)

        sx = sem("sx"); svm = sem("svm"); ssq = sem("ssq"); sxn = sem("sxn")
        stp = sem("stp"); sev_pt = sem("sev_pt")
        smm = sem("smm"); smv = sem("smv"); smt = sem("smt"); smf = sem("smf")
        sqh = sem("sqh"); sqh2 = sem("sqh2"); seh = sem("seh"); seh2 = sem("seh2")
        spv = sem("spv"); sevb = sem("sevb")
        spn = sem("spn"); spb = sem("spb"); spvv = sem("spvv")
        sse = sem("sse"); sRb = sem("sRb")
        sfv = sem("sfv"); sfd = sem("sfd")
        swg = sem("swg"); sg = sem("sg"); ssil = sem("ssil"); sup = sem("sup")
        shs = sem("shs"); scv = sem("scv"); swd = sem("swd"); sdn = sem("sdn")
        sdne = sem("sdne"); sya = sem("sya"); sy = sem("sy")
        sxr = sem("sxr"); sar = sem("sar"); sid = sem("sid")
        vaux = sem("vaux"); saux = sem("saux")

        PT_FFN = 32 + 8 * n_pairs
        NITER = n_pairs * NB * NA

        block = ctx.enter_context(nc.Block())

        from contextlib import contextmanager
        _mlc = [0]

        @contextmanager
        def mloop(r_i, n, eng):
            _mlc[0] += 1
            ls = f"mloop{_mlc[0]}_s"
            le = f"mloop{_mlc[0]}_e"
            nc.regs_mov(r_i, 0)
            nc.br(ls, engines=[eng])
            with nc.body(ls, valid_engines=[eng]):
                yield
                nc.regs_alu(r_i, r_i, 1, op=ALU.add)
                nc.br_lt(r_i, n, on_true=ls, on_false=le, engines=[eng])
            nc.switch_bb(le)

        @block.sync
        def _(sync):
            for i in range(NT):
                if i >= 2:
                    sync.wait_ge(ssq, i - 1)
                    sync.wait_ge(sxn, i - 1)
                sync.dma_start(out=xt[i % 2][:], in_=x_in[i * 128:(i + 1) * 128, :]).then_inc(sx, 16)
            for p in range(n_pairs):
                for fb in range(4):
                    kf = p * 4 + fb
                    sync.wait_ge(sfv, kf + 1)
                    dst = attn_s[fb * 1024:(fb + 1) * 1024, p * 128:(p + 1) * 128]
                    dst = dst.rearrange("(t q) c -> q t c", q=128)
                    src = fin_buf[kf % 2][:].rearrange("q (t c) -> q t c", t=8)
                    sync.dma_start(out=dst, in_=src).then_inc(sfd, 16)
            if debug:
                sync.wait_ge(smf, 1)
                sync.dma_start(out=dbg_xcf, in_=xcf[0][:]).then_inc(sid, 16)
                sync.dma_start(out=dbg_mu, in_=mu_rowb[:]).then_inc(sid, 16)
                if n_pairs >= 1:
                    sync.wait_ge(spn, 1)
                    sync.dma_start(out=dbg_n2, in_=n2[:]).then_inc(sid, 16)
            if do_ffn and ffn_stage >= 3:
                def y_dma(tj):
                    sync.wait_ge(sya, tj + 1)
                    sync.dma_start(out=y_out[tj * 128:(tj + 1) * 128, :],
                                   in_=y_buf[tj % 2][:]).then_inc(sy, 16)
                for ti in range(NT):
                    if ti >= 2:
                        sync.wait_ge(sya, ti - 1)
                    sync.dma_start(out=xt[ti % 2][:],
                                   in_=x_in[ti * 128:(ti + 1) * 128, :]).then_inc(sxr, 16)
                    if ti == 0 and n_pairs:
                        sync.wait_ge(sfd, 16 * 4 * n_pairs)
                    sync.dma_start(out=at_re[ti % 2][:],
                                   in_=attn_s[ti * 128:(ti + 1) * 128, :]).then_inc(sar, 16)
                    if ti >= 2:
                        y_dma(ti - 2)
                y_dma(NT - 2)
                y_dma(NT - 1)
                sync.wait_ge(sy, 16 * NT)
            elif do_ffn and ffn_stage == 1:
                sync.wait_ge(shs, NC * HC)
                if n_pairs:
                    sync.wait_ge(sfd, 16 * 4 * n_pairs)
            elif do_ffn and ffn_stage == 2:
                sync.wait_ge(sev_pt, 32 + 8 * n_pairs + NC * 16)
                if n_pairs:
                    sync.wait_ge(sfd, 16 * 4 * n_pairs)
            else:
                if n_pairs:
                    sync.wait_ge(sfd, 16 * 4 * n_pairs)
                if debug:
                    sync.wait_ge(sid, 48 if n_pairs else 32)

        @block.gpsimd
        def _(g):
            g.dma_start(out=ident[:], in_=ident_in).then_inc(sid, 16)
            if do_ffn:
                for c in range(NC):
                    for h in range(HC):
                        kh = c * HC + h
                        if kh >= 2:
                            g.wait_ge(sg, kh - 1)
                            g.wait_ge(sup, kh - 1)
                        src_g = wgT[:, h * 128:(h + 1) * 128].rearrange(
                            "(dc p) hl -> p dc hl", p=128)
                        src_u = wuT[:, h * 128:(h + 1) * 128].rearrange(
                            "(dc p) hl -> p dc hl", p=128)
                        dst_g = wg_sl[kh % 2][:].rearrange("p (dc hl) -> p dc hl", dc=DC)
                        dst_u = wu_sl[kh % 2][:].rearrange("p (dc hl) -> p dc hl", dc=DC)
                        g.dma_start(out=dst_g, in_=src_g).then_inc(swg, 16)
                        g.dma_start(out=dst_u, in_=src_u).then_inc(swg, 16)
                    for hf in range(2):
                        if ffn_stage < 2:
                            break
                        for d in range(DC):
                            w = (c * 2 + hf) * DC + d
                            if w >= 2:
                                g.wait_ge(sdn, w - 1)
                            src_d = wdT[:, d * 128:(d + 1) * 128].rearrange(
                                "(hc p) dl -> p hc dl", p=128)
                            dst_d = wd_cs[w % 2][:].rearrange("p (hc dl) -> p hc dl", hc=HC)
                            g.dma_start(out=dst_d, in_=src_d).then_inc(swd, 16)

        @block.tensor
        def _(t):
            PE = mybir.EngineType.PE
            r_g = nc.alloc_registers("r_g", engines=[PE])
            r_evb = nc.alloc_registers("r_evb", engines=[PE])
            r_b0 = nc.alloc_registers("r_b0", engines=[PE])
            r_b512 = nc.alloc_registers("r_b512", engines=[PE])
            r_ti = nc.alloc_registers("r_ti", engines=[PE])
            nc.regs_mov(r_g, 0)
            nc.regs_mov(r_evb, 0)
            rv_g = bass_rust.RuntimeValue(r_g)
            rv_evb = bass_rust.RuntimeValue(r_evb)
            t.wait_ge(sid, 16)
            pt_k = [0]

            def pt_start():
                k = pt_k[0]
                if k >= 2:
                    t.wait_ge(sev_pt, k - 1)
                return (k % 2) * 1024

            def pt_end(last):
                last.then_inc(stp, 1)
                pt_k[0] += 1

            for i in range(NT):
                t.wait_ge(sxn, i + 1)
                g0 = pt_start()
                t.matmul(pC[0:1, 0:512], onesc[:], xnb[:, 0:512],
                         start=(i == 0), stop=(i == NT - 1))
                mm = t.matmul(pC[0:1, 512:1024], onesc[:], xnb[:, 512:1024],
                              start=(i == 0), stop=(i == NT - 1))
                if i == NT - 1:
                    mm.then_inc(smm, 1)
                for d in range(DC):
                    li = t.transpose(pT[:, g0 + d * 128:g0 + (d + 1) * 128],
                                     xnb[:, d * 128:(d + 1) * 128], ident[:])
                pt_end(li)

            t.wait_ge(smv, 1)
            for d in range(DC):
                t.matmul(pC[:, d:d + 1], mu_rowb[:, d * 128:(d + 1) * 128],
                         ident[0:1, 0:1], start=True, stop=True)
            t.matmul(pA[:, 0:512], ones1[:], mu_rowb[:, 0:512], start=True, stop=True)
            mm = t.matmul(pA[:, 512:1024], ones1[:], mu_rowb[:, 512:1024], start=True, stop=True)
            mm.then_inc(smt, 1)

            def t_prep(p_):
                if p_ == 0:
                    t.wait_ge(smf, 1)
                else:
                    t.wait_ge(seh, p_ * NB * NA)
                for bt in range(4):
                    g0 = pt_start()
                    for j in range(8):
                        A = bt * 8 + j
                        li = t.transpose(pT[:, g0 + j * 128:g0 + (j + 1) * 128],
                                         xcf[p_][:, A * 128:(A + 1) * 128], ident[:])
                    pt_end(li)

            t_prep(0)
            for p in range(n_pairs):
                t.wait_ge(sev_pt, pt_k[0])
                nc.regs_mov(r_b0, 1024)
                nc.regs_mov(r_b512, 1536)

                def t_qkt(A, boff, boff5, skip2=False):
                    lh = xcf[p][0:64, A * 128:(A + 1) * 128]
                    t.wait_ge(seh, rv_g)
                    t.matmul(pA[:, 0:512], lh,
                             xcf[p][0:64, bass.ds(boff, 512)], start=True, stop=True,
                             tile_position=(0, 0))
                    t.matmul(pA[:, 512:1024], lh,
                             xcf[p][0:64, bass.ds(boff5, 512)],
                             start=True, stop=True,
                             tile_position=(0, 0)).then_inc(sqh, 1)
                    g2 = (A % 2) * 1024
                    lh2 = xcf[p][64:128, A * 128:(A + 1) * 128]
                    if not skip2:
                        t.wait_ge(seh2, rv_g)
                    t.matmul(pB[:, g2:g2 + 512], lh2,
                             xcf[p][64:128, bass.ds(boff, 512)], start=True, stop=True,
                             tile_position=(64, 0))
                    t.matmul(pB[:, g2 + 512:g2 + 1024], lh2,
                             xcf[p][64:128, bass.ds(boff5, 512)],
                             start=True, stop=True,
                             tile_position=(64, 0)).then_inc(sqh2, 1)

                def t_pv(A, st_, sp_):
                    eg = (A % 2) * 1024
                    vA = vt[:, A * 128:A * 128 + 64]
                    vA2 = vt[:, A * 128 + 64:(A + 1) * 128]
                    t.matmul(pC[0:64, 0:512], vA, e_h[0][:, eg:eg + 512],
                             start=st_, stop=sp_, tile_position=(0, 0))
                    t.matmul(pC[0:64, 512:1024], vA, e_h[0][:, eg + 512:eg + 1024],
                             start=st_, stop=sp_, tile_position=(0, 0))
                    t.matmul(pC[64:128, 0:512], vA2, e_h[1][:, eg:eg + 512],
                             start=st_, stop=sp_, tile_position=(0, 64))
                    t.matmul(pC[64:128, 512:1024], vA2, e_h[1][:, eg + 512:eg + 1024],
                             start=st_, stop=sp_, tile_position=(0, 64)).then_inc(spv, 1)

                def t_slot(A, boff, boff5, pv_idx, pv_start, pv_stop):
                    if A == 2:
                        t.wait_ge(sevb, rv_evb)
                        t.scalar_reg_alu(ALU.add, r_evb.handles[0], 1)
                        t.wait_ge(spvv, 4 * (p + 1))
                    if pv_idx is not None:
                        t.wait_ge(seh2, rv_g)
                        t_pv(pv_idx, pv_start, pv_stop)
                    t_qkt(A, boff, boff5, skip2=pv_idx is not None)
                    t.scalar_reg_alu(ALU.add, r_g.handles[0], 1)

                for A in range(NA):
                    t_slot(A, 0, 512, A - 2 if A >= 2 else None, A == 2, False)
                with mloop(r_ti, NB - 1, mybir.EngineType.PE):
                    boff = bass_rust.RuntimeValue(r_b0, min_val=1024, max_val=3072)
                    boff5 = bass_rust.RuntimeValue(r_b512, min_val=1536, max_val=3584)
                    for A in range(NA):
                        if A == 0:
                            pvi, pvs, pvp = NA - 2, False, False
                        elif A == 1:
                            pvi, pvs, pvp = NA - 1, False, True
                        else:
                            pvi, pvs, pvp = A - 2, A == 2, False
                        t_slot(A, boff, boff5, pvi, pvs, pvp)
                    t.scalar_reg_alu(ALU.add, r_b0.handles[0], 1024)
                    t.scalar_reg_alu(ALU.add, r_b512.handles[0], 1024)
                t.scalar_reg_alu(ALU.add, r_g.handles[0], 1)
                t.wait_ge(seh2, rv_g)
                t_pv(NA - 2, False, False)
                t_pv(NA - 1, False, True)
                t.scalar_reg_alu(ALU.add, r_g.handles[0], -1)
                if p + 1 < n_pairs:
                    t_prep(p + 1)
                t.wait_ge(sevb, (p + 1) * NB)
                for bt in range(4):
                    g0 = pt_start()
                    for j in range(8):
                        A = bt * 8 + j
                        li = t.transpose(pT[:, g0 + j * 128:g0 + (j + 1) * 128],
                                         outT[:, A * 128:(A + 1) * 128], ident[:])
                    pt_end(li)

            if do_ffn:
                def ffn1_unit(c, h, kh_w, kh_sil, kh_hs, with_c):
                    t.wait_ge(swg, kh_w)
                    t.wait_ge(ssil, kh_sil)
                    if with_c is not None:
                        t.wait_ge(scv, with_c)
                    gen = h % 2
                    for d in range(DC):
                        lhs = wg_sl[gen][:, d * 128:(d + 1) * 128]
                        t.matmul(pA[:, 0:512], lhs, xcf[d][:, c * 1024:c * 1024 + 512],
                                 start=(d == 0), stop=(d == DC - 1))
                        mm = t.matmul(pA[:, 512:1024], lhs,
                                      xcf[d][:, c * 1024 + 512:(c + 1) * 1024],
                                      start=(d == 0), stop=(d == DC - 1))
                        if with_c is not None:
                            mm = t.matmul(pC[:, 0:1], lhs, muTb[:, d:d + 1],
                                          start=(d == 0), stop=(d == DC - 1))
                    mm.then_inc(sg, 1)
                    t.wait_ge(shs, kh_hs)
                    gb = gen * 1024
                    for d in range(DC):
                        lhs = wu_sl[gen][:, d * 128:(d + 1) * 128]
                        t.matmul(pB[:, gb:gb + 512], lhs, xcf[d][:, c * 1024:c * 1024 + 512],
                                 start=(d == 0), stop=(d == DC - 1))
                        mm = t.matmul(pB[:, gb + 512:gb + 1024], lhs,
                                      xcf[d][:, c * 1024 + 512:(c + 1) * 1024],
                                      start=(d == 0), stop=(d == DC - 1))
                        if with_c is not None:
                            mm = t.matmul(pC[:, 512:513], lhs, muTb[:, d:d + 1],
                                          start=(d == 0), stop=(d == DC - 1))
                    mm.then_inc(sup, 1)

                for c in range(NC):
                    t.wait_ge(sev_pt, pt_k[0])
                    for h in range(HC):
                        kh = c * HC + h
                        ffn1_unit(c, h, 32 * (kh + 1), kh, max(kh - 1, 0),
                                  2 * h if c == 0 else None)
                    if ffn_stage < 2:
                        continue
                    t.wait_ge(shs, (c + 1) * HC)
                    if c == 0:
                        t.wait_ge(scv, 2 * HC)
                    for hf in range(2):
                        for d in range(DC):
                            w = (c * 2 + hf) * DC + d
                            t.wait_ge(swd, 16 * (w + 1))
                            if w >= 2:
                                t.wait_ge(sdne, w - 1)
                            half = (w % 2) * 512
                            for h in range(HC):
                                mm = t.matmul(pC[:, half:half + 512],
                                              wd_cs[w % 2][:, h * 128:(h + 1) * 128],
                                              hs[:, h * 1024 + hf * 512:h * 1024 + (hf + 1) * 512],
                                              start=(h == 0), stop=(h == HC - 1))
                            mm.then_inc(sdn, 1)
                            g0 = pt_start()
                            t.wait_ge(sdne, w + 1)
                            for ts in range(4):
                                li = t.transpose(pT[:, g0 + ts * 128:g0 + (ts + 1) * 128],
                                                 dn_bf[w % 2][:, ts * 128:(ts + 1) * 128],
                                                 ident[:])
                            pt_end(li)

        @block.scalar
        def _(s):
            ACT = mybir.EngineType.Activation
            r_q = nc.alloc_registers("r_q", engines=[ACT])
            r_spv = nc.alloc_registers("r_spv", engines=[ACT])
            r_qb = nc.alloc_registers("r_qb", engines=[ACT])
            r_si = nc.alloc_registers("r_si", engines=[ACT])
            nc.regs_mov(r_q, 1)
            nc.regs_mov(r_spv, 0)
            nc.regs_mov(r_qb, 0)
            rv_q = bass_rust.RuntimeValue(r_q)
            rv_spv = bass_rust.RuntimeValue(r_spv)
            rv_qb = bass_rust.RuntimeValue(r_qb)
            s_aux = [0]

            def sbar(inst):
                s_aux[0] += 1
                inst.then_inc(saux, 1)
                s.wait_ge(saux, s_aux[0])
            for i in range(NT):
                e = i % 2
                s.wait_ge(svm, i + 1)
                sbar(s.activation(trash[:], xt[e][:], AF.Square,
                                  bias=ln_sc[:, e:e + 1], scale=1.0,
                                  accum_out=ln_sc[:, 2 + e:3 + e]))
                s.activation(ln_sc[:, 4 + e:5 + e], ln_sc[:, 2 + e:3 + e], AF.Sqrt,
                             bias=epsv[:], scale=1.0 / D).then_inc(ssq, 1)
            for p in range(n_pairs):
                for bt in range(4):
                    s.wait_ge(spn, 4 * p + bt + 1)
                    s.activation(ebv[:, bt * 8:bt * 8 + 8], n2[:, bt * 8:bt * 8 + 8],
                                 AF.Exp, bias=c30n[:], scale=0.5)
                    s.activation(ebv[:, 32 + bt * 8:32 + bt * 8 + 8],
                                 n2[:, 32 + bt * 8:32 + bt * 8 + 8],
                                 AF.Exp, bias=c30n[:], scale=0.5).then_inc(spb, 1)
                with mloop(r_si, NB, mybir.EngineType.Activation):
                    for A in range(NA):
                        eg = (A % 2) * 1024
                        s.wait_ge(sqh, rv_q)
                        s.wait_ge(spv, rv_spv)
                        if A == 0 or A == 16:
                            s.wait_ge(sRb, rv_qb)
                        if A == 0:
                            s.scalar_reg_alu(ALU.add, r_qb.handles[0], 1)
                        s.activation(e_h[0][:, eg:eg + 1024], pA[:], AF.Exp,
                                     bias=bbias[:, A:A + 1], scale=1.0,
                                     accum_out=rslot[:, A:A + 1]).then_inc(seh, 1)
                        s.wait_ge(sqh2, rv_q)
                        s.activation(e_h[1][:, eg:eg + 1024],
                                     pB[:, (A % 2) * 1024:(A % 2) * 1024 + 1024], AF.Exp,
                                     bias=bbias[:, 32 + A:33 + A], scale=1.0,
                                     accum_out=rslot[:, 32 + A:33 + A]
                                     ).then_inc(seh2, 1)
                        s.scalar_reg_alu(ALU.add, r_q.handles[0], 1)
                        dspv = 0 if A == 0 else (1 if A < NA - 1 else 2)
                        if dspv:
                            s.scalar_reg_alu(ALU.add, r_spv.handles[0], dspv)
                        if A == NA - 1:
                            s.scalar_reg_alu(ALU.add, r_qb.handles[0], 1)
                s.wait_ge(sRb, 1 + 2 * NB * (p + 1))
                s.activation(sval[:, 0:32], n2[:, 0:32], AF.Exp, bias=c30p[:], scale=-0.5)
                s.activation(sval[:, 32:64], n2[:, 32:64], AF.Exp,
                             bias=c30p[:], scale=-0.5).then_inc(sse, 1)
            if do_ffn:
                for c in range(NC):
                    for h in range(HC):
                        kh = c * HC + h
                        s.wait_ge(sg, kh + 1)
                        if kh >= 1:
                            s.wait_ge(shs, kh)
                        if c == 0:
                            s.wait_ge(scv, 2 * h + 1)
                        s.activation(trash[:], pA[:], AF.Silu,
                                     bias=cgs[:, h:h + 1], scale=S127).then_inc(ssil, 1)

        @block.vector
        def _(v):
            pt_c = [0]
            v_aux = [0]

            def vbar(inst):
                v_aux[0] += 1
                inst.then_inc(vaux, 1)
                v.wait_ge(vaux, v_aux[0])

            def ptc_start():
                k = pt_c[0]
                v.wait_ge(stp, k + 1)
                return (k % 2) * 1024

            def ptc_end(last):
                last.then_inc(sev_pt, 1)
                pt_c[0] += 1

            v.memset(ones1[:], 1.0).then_inc(sRb, 1)
            v.memset(onesc[:], 1.0)
            v.memset(epsv[:], EPS)
            v.memset(c30n[:], -30.0)
            v.memset(c30p[:], 30.0)
            v.wait_ge(sx, 16)
            vbar(v.reduce_sum(ln_sc[:, 8:9], xt[0][:], axis=AX.X))
            v.tensor_scalar_mul(ln_sc[:, 0:1], ln_sc[:, 8:9], -1.0 / D).then_inc(svm, 1)
            for i in range(NT):
                if i >= 1:
                    g0 = ptc_start()
                    for d in range(DC):
                        li = v.tensor_copy(xcf[d][:, (i - 1) * 128:i * 128],
                                           pT[:, g0 + d * 128:g0 + (d + 1) * 128])
                    ptc_end(li)
                e = i % 2
                if i + 1 < NT:
                    e2 = (i + 1) % 2
                    v.wait_ge(sx, 16 * (i + 2))
                    vbar(v.reduce_sum(ln_sc[:, 8 + e2:9 + e2], xt[e2][:], axis=AX.X))
                    v.tensor_scalar_mul(ln_sc[:, e2:e2 + 1], ln_sc[:, 8 + e2:9 + e2],
                                        -1.0 / D).then_inc(svm, 1)
                v.wait_ge(ssq, i + 1)
                vbar(v.reciprocal(ln_sc[:, 6 + e:7 + e], ln_sc[:, 4 + e:5 + e]))
                v.tensor_scalar(xnb[:], xt[e][:], ln_sc[:, e:e + 1], ln_sc[:, 6 + e:7 + e],
                                ALU.add, ALU.mult).then_inc(sxn, 1)
            g0 = ptc_start()
            for d in range(DC):
                li = v.tensor_copy(xcf[d][:, (NT - 1) * 128:NT * 128],
                                   pT[:, g0 + d * 128:g0 + (d + 1) * 128])
            ptc_end(li)
            v.wait_ge(smm, 1)
            v.tensor_scalar_mul(mu_rowb[:], pC[0:1, :], 1.0 / T_TOK).then_inc(smv, 1)
            v.wait_ge(smt, 1)
            v.tensor_copy(muT[:], pC[:, 0:DC]).then_inc(seh2, 1)
            v.tensor_copy(muTb[:], pC[:, 0:DC])
            vbar(v.tensor_copy(mu_tile[:], pA[:]))
            for d in range(DC):
                li = v.tensor_scalar_sub(xcf[d][:], xcf[d][:], muT[:, d:d + 1])
            li.then_inc(smf, 1)

            def v_prep(p_):
                if p_ >= 1:
                    v.wait_ge(sse, p_)
                    v.wait_ge(spv, p_ * NB * NA)
                for bt in range(4):
                    g0 = ptc_start()
                    li = v.tensor_copy(vt[:, bt * 1024:(bt + 1) * 1024],
                                       pT[:, g0:g0 + 1024])
                    ptc_end(li)
                    for j in range(8):
                        A = bt * 8 + j
                        v.scalar_tensor_tensor(trash[:, 0:64], vt[:, A * 128:A * 128 + 64],
                                               1.0, vt[:, A * 128:A * 128 + 64],
                                               ALU.mult, ALU.mult,
                                               accum_out=n2[:, A:A + 1])
                        li = v.scalar_tensor_tensor(trash[:, 64:128],
                                                    vt[:, A * 128 + 64:(A + 1) * 128],
                                                    1.0, vt[:, A * 128 + 64:(A + 1) * 128],
                                                    ALU.mult, ALU.mult,
                                                    accum_out=n2[:, 32 + A:33 + A])
                    vbar(li)
                    v.tensor_scalar(bbias[:, bt * 8:bt * 8 + 8], n2[:, bt * 8:bt * 8 + 8],
                                    -0.5, -80.0, ALU.mult, ALU.add)
                    v.tensor_scalar(bbias[:, 32 + bt * 8:32 + bt * 8 + 8],
                                    n2[:, 32 + bt * 8:32 + bt * 8 + 8],
                                    -0.5, -80.0, ALU.mult, ALU.add).then_inc(spn, 1)
                for bt in range(4):
                    v.wait_ge(spb, 4 * p_ + bt + 1)
                    for j in range(8):
                        A = bt * 8 + j
                        v.tensor_scalar_mul(vt[:, A * 128:A * 128 + 64],
                                            vt[:, A * 128:A * 128 + 64], ebv[:, A:A + 1])
                        li = v.tensor_scalar_mul(vt[:, A * 128 + 64:(A + 1) * 128],
                                                 vt[:, A * 128 + 64:(A + 1) * 128],
                                                 ebv[:, 32 + A:33 + A])
                    li.then_inc(spvv, 1)

            v_prep(0)
            for p in range(n_pairs):
                for b in range(NB):
                    K = p * NB + b
                    v.wait_ge(seh2, K * NA + 17)
                    if b == 0:
                        v.tensor_copy(rsum[:, 0:16], rslot[:, 0:16])
                        li = v.tensor_copy(rsum[:, 32:48], rslot[:, 32:48])
                    else:
                        v.tensor_add(rsum[:, 0:16], rsum[:, 0:16], rslot[:, 0:16])
                        li = v.tensor_add(rsum[:, 32:48], rsum[:, 32:48], rslot[:, 32:48])
                    li.then_inc(sRb, 1)
                    v.wait_ge(seh2, (K + 1) * NA + 1)
                    if b == 0:
                        v.tensor_copy(rsum[:, 16:32], rslot[:, 16:32])
                        li = v.tensor_copy(rsum[:, 48:64], rslot[:, 48:64])
                    else:
                        v.tensor_add(rsum[:, 16:32], rsum[:, 16:32], rslot[:, 16:32])
                        li = v.tensor_add(rsum[:, 48:64], rsum[:, 48:64], rslot[:, 48:64])
                    li.then_inc(sRb, 1)
                    v.wait_ge(spv, (K + 1) * NA)
                    v.tensor_copy(outT[:, b * 1024:(b + 1) * 1024], pC[:]).then_inc(sevb, 1)
                v.wait_ge(sRb, 1 + 2 * NB * (p + 1))
                v.wait_ge(sse, p + 1)
                vbar(v.reciprocal(rinv[:], rsum[:]))
                vbar(v.tensor_mul(sval[:], sval[:], rinv[:]))
                if p + 1 < n_pairs:
                    v_prep(p + 1)
                for fb in range(4):
                    g0 = ptc_start()
                    kf = p * 4 + fb
                    if kf >= 2:
                        v.wait_ge(sfd, 16 * (kf - 1))
                    for j in range(8):
                        A = fb * 8 + j
                        v.tensor_scalar_mul(fin_buf[kf % 2][:, j * 128:j * 128 + 64],
                                            pT[:, g0 + j * 128:g0 + j * 128 + 64],
                                            sval[:, A:A + 1])
                        li = v.tensor_scalar_mul(fin_buf[kf % 2][:, j * 128 + 64:(j + 1) * 128],
                                                 pT[:, g0 + j * 128 + 64:g0 + (j + 1) * 128],
                                                 sval[:, 32 + A:33 + A])
                    li.then_inc(sfv, 1)
                    tiny = v.tensor_copy(trash[:, 0:2], fin_buf[kf % 2][:, 0:2])
                    ptc_end(tiny)

            if do_ffn:
                for c in range(NC):
                    for h in range(HC):
                        kh = c * HC + h
                        if c == 0:
                            v.wait_ge(sg, kh + 1)
                            v.tensor_scalar_mul(cgs[:, h:h + 1], pC[:, 0:1], S127).then_inc(scv, 1)
                            v.wait_ge(sup, kh + 1)
                            v.tensor_scalar_mul(cus[:, h:h + 1], pC[:, 512:513], S127).then_inc(scv, 1)
                            v.wait_ge(scv, 2 * h + 2)
                        v.wait_ge(sup, kh + 1)
                        gb = (h % 2) * 1024
                        vbar(v.tensor_scalar(up_b[:], pB[:, gb:gb + 1024], S127, cus[:, h:h + 1],
                                             ALU.mult, ALU.add))
                        v.wait_ge(ssil, kh + 1)
                        v.tensor_mul(hs[:, h * 1024:(h + 1) * 1024], up_b[:],
                                     trash[:]).then_inc(shs, 1)
                    for hf in range(2):
                        if ffn_stage < 2:
                            break
                        for d in range(DC):
                            w = (c * 2 + hf) * DC + d
                            v.wait_ge(sdn, w + 1)
                            if w >= 2:
                                v.wait_ge(stp, PT_FFN + w - 1)
                            half = (w % 2) * 512
                            v.tensor_scalar_mul(dn_bf[w % 2][:], pC[:, half:half + 512],
                                                S127).then_inc(sdne, 1)
                            g0 = ptc_start()
                            for ts in range(4):
                                li = v.tensor_copy(asm[:, ts * D + d * 128:ts * D + (d + 1) * 128],
                                                   pT[:, g0 + ts * 128:g0 + (ts + 1) * 128])
                            ptc_end(li)
                        if ffn_stage < 3:
                            continue
                        v.wait_ge(sev_pt, pt_c[0])
                        for ts in range(4):
                            ti = c * 8 + hf * 4 + ts
                            v.wait_ge(sxr, 16 * (ti + 1))
                            v.wait_ge(sar, 16 * (ti + 1))
                            if ti >= 2:
                                v.wait_ge(sy, 16 * (ti - 1))
                            yb = y_buf[ti % 2][:]
                            vbar(v.tensor_add(yb, xt[ti % 2][:], mu_tile[:]))
                            vbar(v.tensor_add(yb, yb, at_re[ti % 2][:]))
                            v.tensor_add(yb, yb, asm[:, ts * D:(ts + 1) * D]).then_inc(sya, 1)

    return nc


def kernel(L0_attn_x, L0_ffn_w_gate, L0_ffn_w_up, L0_ffn_w_down):
    x = np.asarray(L0_attn_x, dtype=np.float32)
    wg = np.asarray(L0_ffn_w_gate, dtype=np.int32)
    wu = np.asarray(L0_ffn_w_up, dtype=np.int32)
    wd = np.asarray(L0_ffn_w_down, dtype=np.int32)
    wgT = np.ascontiguousarray(wg.T)
    wuT = np.ascontiguousarray(wu.T)
    wdT = np.ascontiguousarray(wd.T)
    ident = np.eye(128, dtype=ml_dtypes.bfloat16)
    nc = build_nc()
    if not nc.is_finalized():
        nc.finalize()
    in_maps = [dict(x=np.ascontiguousarray(x[c]), wgT=wgT, wuT=wuT, wdT=wdT, ident=ident)
               for c in range(8)]
    res = run_bass_kernel_spmd(nc, in_maps, core_ids=list(range(8)))
    return np.stack([np.asarray(res.results[c]["y"]) for c in range(8)], axis=0)


# version 21
# speedup vs baseline: 1.0433x; 1.0433x over previous
import numpy as np
import ml_dtypes
from contextlib import ExitStack

import bass_rust
import concourse.bass as bass
import concourse.bacc as bacc
import concourse.mybir as mybir
from concourse.bass_utils import run_bass_kernel_spmd

F32 = mybir.dt.float32
BF16 = mybir.dt.bfloat16
I32 = mybir.dt.int32
AF = mybir.ActivationFunctionType
ALU = mybir.AluOpType
AX = mybir.AxisListType

T_TOK = 4096
D = 1024
NT = 32
NP = 8
NB = 4
NA = 32
NC = 4
HC = 24
DC = 8
HID = 3072
S127 = float(np.float32(1.0 / 127.0))
EPS = 1e-5


def build_nc(n_pairs=NP, do_ffn=True, debug=False, ffn_stage=3):
    if not do_ffn:
        ffn_stage = 0
    do_ffn = ffn_stage > 0
    nc = bacc.Bacc("TRN2", target_bir_lowering=False)
    x_in = nc.dram_tensor("x", [T_TOK, D], F32, kind="ExternalInput").ap()
    wgT = nc.dram_tensor("wgT", [HC * 128, DC * 128], I32, kind="ExternalInput").ap()
    wuT = nc.dram_tensor("wuT", [HC * 128, DC * 128], I32, kind="ExternalInput").ap()
    wdT = nc.dram_tensor("wdT", [DC * 128, HC * 128], I32, kind="ExternalInput").ap()
    ident_in = nc.dram_tensor("ident", [128, 128], BF16, kind="ExternalInput").ap()
    y_out = nc.dram_tensor("y", [T_TOK, D], F32, kind="ExternalOutput").ap()
    attn_kind = "ExternalOutput" if debug else "Internal"
    attn_s = nc.dram_tensor("attn_s", [T_TOK, D], BF16, kind=attn_kind).ap()
    if debug:
        dbg_xcf = nc.dram_tensor("dbg_xcf", [128, T_TOK], BF16, kind="ExternalOutput").ap()
        dbg_mu = nc.dram_tensor("dbg_mu", [1, D], BF16, kind="ExternalOutput").ap()
        dbg_n2 = nc.dram_tensor("dbg_n2", [128, 64], F32, kind="ExternalOutput").ap()

    ctx = ExitStack()
    sb = lambda name, shape, dt: ctx.enter_context(nc.sbuf_tensor(name, shape, dt))
    ps = lambda name, shape, dt: ctx.enter_context(nc.psum_tensor(name, shape, dt))
    sem = lambda name: ctx.enter_context(nc.semaphore(name))

    with ctx:
        xcf = [sb(f"xcf{d}", [128, T_TOK], BF16) for d in range(DC)]
        xt = [sb(f"xt{i}", [128, D], F32) for i in range(2)]
        xnb = sb("xnb", [128, D], BF16)
        trash = sb("trash", [128, D], BF16)
        ln_sc = sb("ln_sc", [128, 10], F32)
        epsv = sb("epsv", [128, 1], F32)
        c30n = sb("c30n", [128, 1], F32)
        c30p = sb("c30p", [128, 1], F32)
        ident = sb("identS", [128, 128], BF16)
        ones1 = sb("ones1", [1, 128], BF16)
        onesc = sb("onesc", [128, 1], BF16)
        mu_rowb = sb("mu_rowb", [1, D], BF16)
        muT = sb("muT", [128, DC], F32)
        muTb = sb("muTb", [128, DC], BF16)
        mu_tile = sb("mu_tile", [128, D], BF16)
        vt = sb("vt", [128, T_TOK], BF16)
        outT = sb("outT", [128, T_TOK], BF16)
        e_h = [sb(f"e_h{k}", [128, 2048], BF16) for k in range(2)]
        n2 = sb("n2", [128, 64], F32)
        bbias = sb("bbias", [128, 64], F32)
        ebv = sb("ebv", [128, 64], F32)
        rslot = sb("rslot", [128, 64], F32)
        rsum = sb("rsum", [128, 64], F32)
        rinv = sb("rinv", [128, 64], F32)
        sval = sb("sval", [128, 64], F32)
        fin_buf = [sb(f"fin{k}", [128, 1024], BF16) for k in range(2)]
        wg_sl = [sb(f"wg_sl{k}", [128, 1024], BF16) for k in range(2)]
        wu_sl = [sb(f"wu_sl{k}", [128, 1024], BF16) for k in range(2)]
        wd_cs = [sb(f"wd_cs{k}", [128, HC * 128], BF16) for k in range(2)]
        up_b = sb("up_b", [128, 1024], BF16)
        hs = sb("hs", [128, HC * 1024], BF16)
        cgs = sb("cgs", [128, HC], F32)
        cus = sb("cus", [128, HC], F32)
        dn_bf = [sb(f"dn_bf{k}", [128, 512], BF16) for k in range(2)]
        asm = sb("asm", [128, 4 * D], BF16)
        at_re = [sb(f"at_re{k}", [128, D], BF16) for k in range(2)]
        y_buf = [sb(f"y_buf{k}", [128, D], F32) for k in range(2)]

        pA = ps("pA", [128, 1024], F32)
        pB = ps("pB", [128, 2048], F32)
        pC = ps("pC", [128, 1024], F32)
        pT = pA.bitcast(BF16)

        sx = sem("sx"); svm = sem("svm"); ssq = sem("ssq"); sxn = sem("sxn")
        stp = sem("stp"); sev_pt = sem("sev_pt")
        smm = sem("smm"); smv = sem("smv"); smt = sem("smt"); smf = sem("smf")
        sqh = sem("sqh"); sqh2 = sem("sqh2"); seh = sem("seh"); seh2 = sem("seh2")
        spv = sem("spv"); sevb = sem("sevb")
        spn = sem("spn"); spb = sem("spb"); spvv = sem("spvv")
        sse = sem("sse"); sRb = sem("sRb")
        sfv = sem("sfv"); sfd = sem("sfd")
        swg = sem("swg"); sg = sem("sg"); ssil = sem("ssil"); sup = sem("sup")
        shs = sem("shs"); scv = sem("scv"); swd = sem("swd"); sdn = sem("sdn")
        sdne = sem("sdne"); sya = sem("sya"); sy = sem("sy")
        sxr = sem("sxr"); sar = sem("sar"); sid = sem("sid")
        vaux = sem("vaux"); saux = sem("saux")

        PT_FFN = 32 + 8 * n_pairs
        NITER = n_pairs * NB * NA

        block = ctx.enter_context(nc.Block())

        from contextlib import contextmanager
        _mlc = [0]

        @contextmanager
        def mloop(r_i, n, eng):
            _mlc[0] += 1
            ls = f"mloop{_mlc[0]}_s"
            le = f"mloop{_mlc[0]}_e"
            nc.regs_mov(r_i, 0)
            nc.br(ls, engines=[eng])
            with nc.body(ls, valid_engines=[eng]):
                yield
                nc.regs_alu(r_i, r_i, 1, op=ALU.add)
                nc.br_lt(r_i, n, on_true=ls, on_false=le, engines=[eng])
            nc.switch_bb(le)

        @block.sync
        def _(sync):
            for i in range(NT):
                if i >= 2:
                    sync.wait_ge(ssq, i - 1)
                    sync.wait_ge(sxn, i - 1)
                sync.dma_start(out=xt[i % 2][:], in_=x_in[i * 128:(i + 1) * 128, :]).then_inc(sx, 16)
            for p in range(n_pairs):
                for fb in range(4):
                    kf = p * 4 + fb
                    sync.wait_ge(sfv, kf + 1)
                    dst = attn_s[fb * 1024:(fb + 1) * 1024, p * 128:(p + 1) * 128]
                    dst = dst.rearrange("(t q) c -> q t c", q=128)
                    src = fin_buf[kf % 2][:].rearrange("q (t c) -> q t c", t=8)
                    sync.dma_start(out=dst, in_=src).then_inc(sfd, 16)
            if debug:
                sync.wait_ge(smf, 1)
                sync.dma_start(out=dbg_xcf, in_=xcf[0][:]).then_inc(sid, 16)
                sync.dma_start(out=dbg_mu, in_=mu_rowb[:]).then_inc(sid, 16)
                if n_pairs >= 1:
                    sync.wait_ge(spn, 1)
                    sync.dma_start(out=dbg_n2, in_=n2[:]).then_inc(sid, 16)
            if do_ffn and ffn_stage >= 3:
                def y_dma(tj):
                    sync.wait_ge(sya, tj + 1)
                    sync.dma_start(out=y_out[tj * 128:(tj + 1) * 128, :],
                                   in_=y_buf[tj % 2][:]).then_inc(sy, 16)
                for ti in range(NT):
                    if ti >= 2:
                        sync.wait_ge(sya, ti - 1)
                    sync.dma_start(out=xt[ti % 2][:],
                                   in_=x_in[ti * 128:(ti + 1) * 128, :]).then_inc(sxr, 16)
                    if ti == 0 and n_pairs:
                        sync.wait_ge(sfd, 16 * 4 * n_pairs)
                    sync.dma_start(out=at_re[ti % 2][:],
                                   in_=attn_s[ti * 128:(ti + 1) * 128, :]).then_inc(sar, 16)
                    if ti >= 2:
                        y_dma(ti - 2)
                y_dma(NT - 2)
                y_dma(NT - 1)
                sync.wait_ge(sy, 16 * NT)
            elif do_ffn and ffn_stage == 1:
                sync.wait_ge(shs, NC * HC)
                if n_pairs:
                    sync.wait_ge(sfd, 16 * 4 * n_pairs)
            elif do_ffn and ffn_stage == 2:
                sync.wait_ge(sev_pt, 32 + 8 * n_pairs + NC * 16)
                if n_pairs:
                    sync.wait_ge(sfd, 16 * 4 * n_pairs)
            else:
                if n_pairs:
                    sync.wait_ge(sfd, 16 * 4 * n_pairs)
                if debug:
                    sync.wait_ge(sid, 48 if n_pairs else 32)

        @block.gpsimd
        def _(g):
            g.dma_start(out=ident[:], in_=ident_in).then_inc(sid, 16)
            if do_ffn:
                for c in range(NC):
                    for h in range(HC):
                        kh = c * HC + h
                        if kh >= 2:
                            g.wait_ge(sg, kh - 1)
                            g.wait_ge(sup, kh - 1)
                        g.dma_start(out=wg_sl[kh % 2][:],
                                    in_=wgT[h * 128:(h + 1) * 128, :]).then_inc(swg, 16)
                        g.dma_start(out=wu_sl[kh % 2][:],
                                    in_=wuT[h * 128:(h + 1) * 128, :]).then_inc(swg, 16)
                    for hf in range(2):
                        if ffn_stage < 2:
                            break
                        for d in range(DC):
                            w = (c * 2 + hf) * DC + d
                            if w >= 2:
                                g.wait_ge(sdn, w - 1)
                            g.dma_start(out=wd_cs[w % 2][:],
                                        in_=wdT[d * 128:(d + 1) * 128, :]).then_inc(swd, 16)

        @block.tensor
        def _(t):
            PE = mybir.EngineType.PE
            r_g = nc.alloc_registers("r_g", engines=[PE])
            r_evb = nc.alloc_registers("r_evb", engines=[PE])
            r_b0 = nc.alloc_registers("r_b0", engines=[PE])
            r_b512 = nc.alloc_registers("r_b512", engines=[PE])
            r_ti = nc.alloc_registers("r_ti", engines=[PE])
            nc.regs_mov(r_g, 0)
            nc.regs_mov(r_evb, 0)
            rv_g = bass_rust.RuntimeValue(r_g)
            rv_evb = bass_rust.RuntimeValue(r_evb)
            t.wait_ge(sid, 16)
            pt_k = [0]

            def pt_start():
                k = pt_k[0]
                if k >= 2:
                    t.wait_ge(sev_pt, k - 1)
                return (k % 2) * 1024

            def pt_end(last):
                last.then_inc(stp, 1)
                pt_k[0] += 1

            for i in range(NT):
                t.wait_ge(sxn, i + 1)
                g0 = pt_start()
                t.matmul(pC[0:1, 0:512], onesc[:], xnb[:, 0:512],
                         start=(i == 0), stop=(i == NT - 1))
                mm = t.matmul(pC[0:1, 512:1024], onesc[:], xnb[:, 512:1024],
                              start=(i == 0), stop=(i == NT - 1))
                if i == NT - 1:
                    mm.then_inc(smm, 1)
                for d in range(DC):
                    li = t.transpose(pT[:, g0 + d * 128:g0 + (d + 1) * 128],
                                     xnb[:, d * 128:(d + 1) * 128], ident[:])
                pt_end(li)

            t.wait_ge(smv, 1)
            for d in range(DC):
                t.matmul(pC[:, d:d + 1], mu_rowb[:, d * 128:(d + 1) * 128],
                         ident[0:1, 0:1], start=True, stop=True)
            t.matmul(pA[:, 0:512], ones1[:], mu_rowb[:, 0:512], start=True, stop=True)
            mm = t.matmul(pA[:, 512:1024], ones1[:], mu_rowb[:, 512:1024], start=True, stop=True)
            mm.then_inc(smt, 1)

            def t_prep(p_):
                if p_ == 0:
                    t.wait_ge(smf, 1)
                else:
                    t.wait_ge(seh, p_ * NB * NA)
                for bt in range(4):
                    g0 = pt_start()
                    for j in range(8):
                        A = bt * 8 + j
                        li = t.transpose(pT[:, g0 + j * 128:g0 + (j + 1) * 128],
                                         xcf[p_][:, A * 128:(A + 1) * 128], ident[:])
                    pt_end(li)

            t_prep(0)
            for p in range(n_pairs):
                t.wait_ge(sev_pt, pt_k[0])
                nc.regs_mov(r_b0, 1024)
                nc.regs_mov(r_b512, 1536)

                def t_qkt(A, boff, boff5, skip2=False):
                    lh = xcf[p][0:64, A * 128:(A + 1) * 128]
                    t.wait_ge(seh, rv_g)
                    t.matmul(pA[:, 0:512], lh,
                             xcf[p][0:64, bass.ds(boff, 512)], start=True, stop=True,
                             tile_position=(0, 0))
                    t.matmul(pA[:, 512:1024], lh,
                             xcf[p][0:64, bass.ds(boff5, 512)],
                             start=True, stop=True,
                             tile_position=(0, 0)).then_inc(sqh, 1)
                    g2 = (A % 2) * 1024
                    lh2 = xcf[p][64:128, A * 128:(A + 1) * 128]
                    if not skip2:
                        t.wait_ge(seh2, rv_g)
                    t.matmul(pB[:, g2:g2 + 512], lh2,
                             xcf[p][64:128, bass.ds(boff, 512)], start=True, stop=True,
                             tile_position=(64, 0))
                    t.matmul(pB[:, g2 + 512:g2 + 1024], lh2,
                             xcf[p][64:128, bass.ds(boff5, 512)],
                             start=True, stop=True,
                             tile_position=(64, 0)).then_inc(sqh2, 1)

                def t_pv(A, st_, sp_):
                    eg = (A % 2) * 1024
                    vA = vt[:, A * 128:A * 128 + 64]
                    vA2 = vt[:, A * 128 + 64:(A + 1) * 128]
                    t.matmul(pC[0:64, 0:512], vA, e_h[0][:, eg:eg + 512],
                             start=st_, stop=sp_, tile_position=(0, 0))
                    t.matmul(pC[0:64, 512:1024], vA, e_h[0][:, eg + 512:eg + 1024],
                             start=st_, stop=sp_, tile_position=(0, 0))
                    t.matmul(pC[64:128, 0:512], vA2, e_h[1][:, eg:eg + 512],
                             start=st_, stop=sp_, tile_position=(0, 64))
                    t.matmul(pC[64:128, 512:1024], vA2, e_h[1][:, eg + 512:eg + 1024],
                             start=st_, stop=sp_, tile_position=(0, 64)).then_inc(spv, 1)

                def t_slot(A, boff, boff5, pv_idx, pv_start, pv_stop):
                    if A == 2:
                        t.wait_ge(sevb, rv_evb)
                        t.scalar_reg_alu(ALU.add, r_evb.handles[0], 1)
                        t.wait_ge(spvv, 4 * (p + 1))
                    if pv_idx is not None:
                        t.wait_ge(seh2, rv_g)
                        t_pv(pv_idx, pv_start, pv_stop)
                    t_qkt(A, boff, boff5, skip2=pv_idx is not None)
                    t.scalar_reg_alu(ALU.add, r_g.handles[0], 1)

                for A in range(NA):
                    t_slot(A, 0, 512, A - 2 if A >= 2 else None, A == 2, False)
                with mloop(r_ti, NB - 1, mybir.EngineType.PE):
                    boff = bass_rust.RuntimeValue(r_b0, min_val=1024, max_val=3072)
                    boff5 = bass_rust.RuntimeValue(r_b512, min_val=1536, max_val=3584)
                    for A in range(NA):
                        if A == 0:
                            pvi, pvs, pvp = NA - 2, False, False
                        elif A == 1:
                            pvi, pvs, pvp = NA - 1, False, True
                        else:
                            pvi, pvs, pvp = A - 2, A == 2, False
                        t_slot(A, boff, boff5, pvi, pvs, pvp)
                    t.scalar_reg_alu(ALU.add, r_b0.handles[0], 1024)
                    t.scalar_reg_alu(ALU.add, r_b512.handles[0], 1024)
                t.scalar_reg_alu(ALU.add, r_g.handles[0], 1)
                t.wait_ge(seh2, rv_g)
                t_pv(NA - 2, False, False)
                t_pv(NA - 1, False, True)
                t.scalar_reg_alu(ALU.add, r_g.handles[0], -1)
                if p + 1 < n_pairs:
                    t_prep(p + 1)
                t.wait_ge(sevb, (p + 1) * NB)
                for bt in range(4):
                    g0 = pt_start()
                    for j in range(8):
                        A = bt * 8 + j
                        li = t.transpose(pT[:, g0 + j * 128:g0 + (j + 1) * 128],
                                         outT[:, A * 128:(A + 1) * 128], ident[:])
                    pt_end(li)

            if do_ffn:
                def ffn1_unit(c, h, kh_w, kh_sil, kh_hs, with_c):
                    t.wait_ge(swg, kh_w)
                    t.wait_ge(ssil, kh_sil)
                    if with_c is not None:
                        t.wait_ge(scv, with_c)
                    gen = h % 2
                    for d in range(DC):
                        lhs = wg_sl[gen][:, d * 128:(d + 1) * 128]
                        t.matmul(pA[:, 0:512], lhs, xcf[d][:, c * 1024:c * 1024 + 512],
                                 start=(d == 0), stop=(d == DC - 1))
                        mm = t.matmul(pA[:, 512:1024], lhs,
                                      xcf[d][:, c * 1024 + 512:(c + 1) * 1024],
                                      start=(d == 0), stop=(d == DC - 1))
                        if with_c is not None:
                            mm = t.matmul(pC[:, 0:1], lhs, muTb[:, d:d + 1],
                                          start=(d == 0), stop=(d == DC - 1))
                    mm.then_inc(sg, 1)
                    t.wait_ge(shs, kh_hs)
                    gb = gen * 1024
                    for d in range(DC):
                        lhs = wu_sl[gen][:, d * 128:(d + 1) * 128]
                        t.matmul(pB[:, gb:gb + 512], lhs, xcf[d][:, c * 1024:c * 1024 + 512],
                                 start=(d == 0), stop=(d == DC - 1))
                        mm = t.matmul(pB[:, gb + 512:gb + 1024], lhs,
                                      xcf[d][:, c * 1024 + 512:(c + 1) * 1024],
                                      start=(d == 0), stop=(d == DC - 1))
                        if with_c is not None:
                            mm = t.matmul(pC[:, 512:513], lhs, muTb[:, d:d + 1],
                                          start=(d == 0), stop=(d == DC - 1))
                    mm.then_inc(sup, 1)

                for c in range(NC):
                    t.wait_ge(sev_pt, pt_k[0])
                    for h in range(HC):
                        kh = c * HC + h
                        ffn1_unit(c, h, 32 * (kh + 1), kh, max(kh - 1, 0),
                                  2 * h if c == 0 else None)
                    if ffn_stage < 2:
                        continue
                    t.wait_ge(shs, (c + 1) * HC)
                    if c == 0:
                        t.wait_ge(scv, 2 * HC)
                    for hf in range(2):
                        for d in range(DC):
                            w = (c * 2 + hf) * DC + d
                            t.wait_ge(swd, 16 * (w + 1))
                            if w >= 2:
                                t.wait_ge(sdne, w - 1)
                            half = (w % 2) * 512
                            for h in range(HC):
                                mm = t.matmul(pC[:, half:half + 512],
                                              wd_cs[w % 2][:, h * 128:(h + 1) * 128],
                                              hs[:, h * 1024 + hf * 512:h * 1024 + (hf + 1) * 512],
                                              start=(h == 0), stop=(h == HC - 1))
                            mm.then_inc(sdn, 1)
                            g0 = pt_start()
                            t.wait_ge(sdne, w + 1)
                            for ts in range(4):
                                li = t.transpose(pT[:, g0 + ts * 128:g0 + (ts + 1) * 128],
                                                 dn_bf[w % 2][:, ts * 128:(ts + 1) * 128],
                                                 ident[:])
                            pt_end(li)

        @block.scalar
        def _(s):
            ACT = mybir.EngineType.Activation
            r_q = nc.alloc_registers("r_q", engines=[ACT])
            r_spv = nc.alloc_registers("r_spv", engines=[ACT])
            r_qb = nc.alloc_registers("r_qb", engines=[ACT])
            r_si = nc.alloc_registers("r_si", engines=[ACT])
            nc.regs_mov(r_q, 1)
            nc.regs_mov(r_spv, 0)
            nc.regs_mov(r_qb, 0)
            rv_q = bass_rust.RuntimeValue(r_q)
            rv_spv = bass_rust.RuntimeValue(r_spv)
            rv_qb = bass_rust.RuntimeValue(r_qb)
            s_aux = [0]

            def sbar(inst):
                s_aux[0] += 1
                inst.then_inc(saux, 1)
                s.wait_ge(saux, s_aux[0])
            for i in range(NT):
                e = i % 2
                s.wait_ge(svm, i + 1)
                sbar(s.activation(trash[:], xt[e][:], AF.Square,
                                  bias=ln_sc[:, e:e + 1], scale=1.0,
                                  accum_out=ln_sc[:, 2 + e:3 + e]))
                s.activation(ln_sc[:, 4 + e:5 + e], ln_sc[:, 2 + e:3 + e], AF.Sqrt,
                             bias=epsv[:], scale=1.0 / D).then_inc(ssq, 1)
            for p in range(n_pairs):
                for bt in range(4):
                    s.wait_ge(spn, 4 * p + bt + 1)
                    s.activation(ebv[:, bt * 8:bt * 8 + 8], n2[:, bt * 8:bt * 8 + 8],
                                 AF.Exp, bias=c30n[:], scale=0.5)
                    s.activation(ebv[:, 32 + bt * 8:32 + bt * 8 + 8],
                                 n2[:, 32 + bt * 8:32 + bt * 8 + 8],
                                 AF.Exp, bias=c30n[:], scale=0.5).then_inc(spb, 1)
                with mloop(r_si, NB, mybir.EngineType.Activation):
                    for A in range(NA):
                        eg = (A % 2) * 1024
                        s.wait_ge(sqh, rv_q)
                        s.wait_ge(spv, rv_spv)
                        if A == 0 or A == 16:
                            s.wait_ge(sRb, rv_qb)
                        if A == 0:
                            s.scalar_reg_alu(ALU.add, r_qb.handles[0], 1)
                        s.activation(e_h[0][:, eg:eg + 1024], pA[:], AF.Exp,
                                     bias=bbias[:, A:A + 1], scale=1.0,
                                     accum_out=rslot[:, A:A + 1]).then_inc(seh, 1)
                        s.wait_ge(sqh2, rv_q)
                        s.activation(e_h[1][:, eg:eg + 1024],
                                     pB[:, (A % 2) * 1024:(A % 2) * 1024 + 1024], AF.Exp,
                                     bias=bbias[:, 32 + A:33 + A], scale=1.0,
                                     accum_out=rslot[:, 32 + A:33 + A]
                                     ).then_inc(seh2, 1)
                        s.scalar_reg_alu(ALU.add, r_q.handles[0], 1)
                        dspv = 0 if A == 0 else (1 if A < NA - 1 else 2)
                        if dspv:
                            s.scalar_reg_alu(ALU.add, r_spv.handles[0], dspv)
                        if A == NA - 1:
                            s.scalar_reg_alu(ALU.add, r_qb.handles[0], 1)
                s.wait_ge(sRb, 1 + 2 * NB * (p + 1))
                s.activation(sval[:, 0:32], n2[:, 0:32], AF.Exp, bias=c30p[:], scale=-0.5)
                s.activation(sval[:, 32:64], n2[:, 32:64], AF.Exp,
                             bias=c30p[:], scale=-0.5).then_inc(sse, 1)
            if do_ffn:
                for c in range(NC):
                    for h in range(HC):
                        kh = c * HC + h
                        s.wait_ge(sg, kh + 1)
                        if kh >= 1:
                            s.wait_ge(shs, kh)
                        if c == 0:
                            s.wait_ge(scv, 2 * h + 1)
                        s.activation(trash[:], pA[:], AF.Silu,
                                     bias=cgs[:, h:h + 1], scale=S127).then_inc(ssil, 1)

        @block.vector
        def _(v):
            pt_c = [0]
            v_aux = [0]

            def vbar(inst):
                v_aux[0] += 1
                inst.then_inc(vaux, 1)
                v.wait_ge(vaux, v_aux[0])

            def ptc_start():
                k = pt_c[0]
                v.wait_ge(stp, k + 1)
                return (k % 2) * 1024

            def ptc_end(last):
                last.then_inc(sev_pt, 1)
                pt_c[0] += 1

            v.memset(ones1[:], 1.0).then_inc(sRb, 1)
            v.memset(onesc[:], 1.0)
            v.memset(epsv[:], EPS)
            v.memset(c30n[:], -30.0)
            v.memset(c30p[:], 30.0)
            v.wait_ge(sx, 16)
            vbar(v.reduce_sum(ln_sc[:, 8:9], xt[0][:], axis=AX.X))
            v.tensor_scalar_mul(ln_sc[:, 0:1], ln_sc[:, 8:9], -1.0 / D).then_inc(svm, 1)
            for i in range(NT):
                if i >= 1:
                    g0 = ptc_start()
                    for d in range(DC):
                        li = v.tensor_copy(xcf[d][:, (i - 1) * 128:i * 128],
                                           pT[:, g0 + d * 128:g0 + (d + 1) * 128])
                    ptc_end(li)
                e = i % 2
                if i + 1 < NT:
                    e2 = (i + 1) % 2
                    v.wait_ge(sx, 16 * (i + 2))
                    vbar(v.reduce_sum(ln_sc[:, 8 + e2:9 + e2], xt[e2][:], axis=AX.X))
                    v.tensor_scalar_mul(ln_sc[:, e2:e2 + 1], ln_sc[:, 8 + e2:9 + e2],
                                        -1.0 / D).then_inc(svm, 1)
                v.wait_ge(ssq, i + 1)
                vbar(v.reciprocal(ln_sc[:, 6 + e:7 + e], ln_sc[:, 4 + e:5 + e]))
                v.tensor_scalar(xnb[:], xt[e][:], ln_sc[:, e:e + 1], ln_sc[:, 6 + e:7 + e],
                                ALU.add, ALU.mult).then_inc(sxn, 1)
            g0 = ptc_start()
            for d in range(DC):
                li = v.tensor_copy(xcf[d][:, (NT - 1) * 128:NT * 128],
                                   pT[:, g0 + d * 128:g0 + (d + 1) * 128])
            ptc_end(li)
            v.wait_ge(smm, 1)
            v.tensor_scalar_mul(mu_rowb[:], pC[0:1, :], 1.0 / T_TOK).then_inc(smv, 1)
            v.wait_ge(smt, 1)
            v.tensor_copy(muT[:], pC[:, 0:DC]).then_inc(seh2, 1)
            v.tensor_copy(muTb[:], pC[:, 0:DC])
            vbar(v.tensor_copy(mu_tile[:], pA[:]))
            for d in range(DC):
                li = v.tensor_scalar_sub(xcf[d][:], xcf[d][:], muT[:, d:d + 1])
            li.then_inc(smf, 1)

            def v_prep(p_):
                if p_ >= 1:
                    v.wait_ge(sse, p_)
                    v.wait_ge(spv, p_ * NB * NA)
                for bt in range(4):
                    g0 = ptc_start()
                    li = v.tensor_copy(vt[:, bt * 1024:(bt + 1) * 1024],
                                       pT[:, g0:g0 + 1024])
                    ptc_end(li)
                    for j in range(8):
                        A = bt * 8 + j
                        v.scalar_tensor_tensor(trash[:, 0:64], vt[:, A * 128:A * 128 + 64],
                                               1.0, vt[:, A * 128:A * 128 + 64],
                                               ALU.mult, ALU.mult,
                                               accum_out=n2[:, A:A + 1])
                        li = v.scalar_tensor_tensor(trash[:, 64:128],
                                                    vt[:, A * 128 + 64:(A + 1) * 128],
                                                    1.0, vt[:, A * 128 + 64:(A + 1) * 128],
                                                    ALU.mult, ALU.mult,
                                                    accum_out=n2[:, 32 + A:33 + A])
                    vbar(li)
                    v.tensor_scalar(bbias[:, bt * 8:bt * 8 + 8], n2[:, bt * 8:bt * 8 + 8],
                                    -0.5, -80.0, ALU.mult, ALU.add)
                    v.tensor_scalar(bbias[:, 32 + bt * 8:32 + bt * 8 + 8],
                                    n2[:, 32 + bt * 8:32 + bt * 8 + 8],
                                    -0.5, -80.0, ALU.mult, ALU.add).then_inc(spn, 1)
                for bt in range(4):
                    v.wait_ge(spb, 4 * p_ + bt + 1)
                    for j in range(8):
                        A = bt * 8 + j
                        v.tensor_scalar_mul(vt[:, A * 128:A * 128 + 64],
                                            vt[:, A * 128:A * 128 + 64], ebv[:, A:A + 1])
                        li = v.tensor_scalar_mul(vt[:, A * 128 + 64:(A + 1) * 128],
                                                 vt[:, A * 128 + 64:(A + 1) * 128],
                                                 ebv[:, 32 + A:33 + A])
                    li.then_inc(spvv, 1)

            v_prep(0)
            for p in range(n_pairs):
                for b in range(NB):
                    K = p * NB + b
                    v.wait_ge(seh2, K * NA + 17)
                    if b == 0:
                        v.tensor_copy(rsum[:, 0:16], rslot[:, 0:16])
                        li = v.tensor_copy(rsum[:, 32:48], rslot[:, 32:48])
                    else:
                        v.tensor_add(rsum[:, 0:16], rsum[:, 0:16], rslot[:, 0:16])
                        li = v.tensor_add(rsum[:, 32:48], rsum[:, 32:48], rslot[:, 32:48])
                    li.then_inc(sRb, 1)
                    v.wait_ge(seh2, (K + 1) * NA + 1)
                    if b == 0:
                        v.tensor_copy(rsum[:, 16:32], rslot[:, 16:32])
                        li = v.tensor_copy(rsum[:, 48:64], rslot[:, 48:64])
                    else:
                        v.tensor_add(rsum[:, 16:32], rsum[:, 16:32], rslot[:, 16:32])
                        li = v.tensor_add(rsum[:, 48:64], rsum[:, 48:64], rslot[:, 48:64])
                    li.then_inc(sRb, 1)
                    v.wait_ge(spv, (K + 1) * NA)
                    v.tensor_copy(outT[:, b * 1024:(b + 1) * 1024], pC[:]).then_inc(sevb, 1)
                v.wait_ge(sRb, 1 + 2 * NB * (p + 1))
                v.wait_ge(sse, p + 1)
                vbar(v.reciprocal(rinv[:], rsum[:]))
                vbar(v.tensor_mul(sval[:], sval[:], rinv[:]))
                if p + 1 < n_pairs:
                    v_prep(p + 1)
                for fb in range(4):
                    g0 = ptc_start()
                    kf = p * 4 + fb
                    if kf >= 2:
                        v.wait_ge(sfd, 16 * (kf - 1))
                    for j in range(8):
                        A = fb * 8 + j
                        v.tensor_scalar_mul(fin_buf[kf % 2][:, j * 128:j * 128 + 64],
                                            pT[:, g0 + j * 128:g0 + j * 128 + 64],
                                            sval[:, A:A + 1])
                        li = v.tensor_scalar_mul(fin_buf[kf % 2][:, j * 128 + 64:(j + 1) * 128],
                                                 pT[:, g0 + j * 128 + 64:g0 + (j + 1) * 128],
                                                 sval[:, 32 + A:33 + A])
                    li.then_inc(sfv, 1)
                    tiny = v.tensor_copy(trash[:, 0:2], fin_buf[kf % 2][:, 0:2])
                    ptc_end(tiny)

            if do_ffn:
                for c in range(NC):
                    for h in range(HC):
                        kh = c * HC + h
                        if c == 0:
                            v.wait_ge(sg, kh + 1)
                            v.tensor_scalar_mul(cgs[:, h:h + 1], pC[:, 0:1], S127).then_inc(scv, 1)
                            v.wait_ge(sup, kh + 1)
                            v.tensor_scalar_mul(cus[:, h:h + 1], pC[:, 512:513], S127).then_inc(scv, 1)
                            v.wait_ge(scv, 2 * h + 2)
                        v.wait_ge(sup, kh + 1)
                        gb = (h % 2) * 1024
                        vbar(v.tensor_scalar(up_b[:], pB[:, gb:gb + 1024], S127, cus[:, h:h + 1],
                                             ALU.mult, ALU.add))
                        v.wait_ge(ssil, kh + 1)
                        v.tensor_mul(hs[:, h * 1024:(h + 1) * 1024], up_b[:],
                                     trash[:]).then_inc(shs, 1)
                    for hf in range(2):
                        if ffn_stage < 2:
                            break
                        for d in range(DC):
                            w = (c * 2 + hf) * DC + d
                            v.wait_ge(sdn, w + 1)
                            if w >= 2:
                                v.wait_ge(stp, PT_FFN + w - 1)
                            half = (w % 2) * 512
                            v.tensor_scalar_mul(dn_bf[w % 2][:], pC[:, half:half + 512],
                                                S127).then_inc(sdne, 1)
                            g0 = ptc_start()
                            for ts in range(4):
                                li = v.tensor_copy(asm[:, ts * D + d * 128:ts * D + (d + 1) * 128],
                                                   pT[:, g0 + ts * 128:g0 + (ts + 1) * 128])
                            ptc_end(li)
                        if ffn_stage < 3:
                            continue
                        v.wait_ge(sev_pt, pt_c[0])
                        for ts in range(4):
                            ti = c * 8 + hf * 4 + ts
                            v.wait_ge(sxr, 16 * (ti + 1))
                            v.wait_ge(sar, 16 * (ti + 1))
                            if ti >= 2:
                                v.wait_ge(sy, 16 * (ti - 1))
                            yb = y_buf[ti % 2][:]
                            vbar(v.tensor_add(yb, xt[ti % 2][:], mu_tile[:]))
                            vbar(v.tensor_add(yb, yb, at_re[ti % 2][:]))
                            v.tensor_add(yb, yb, asm[:, ts * D:(ts + 1) * D]).then_inc(sya, 1)

    return nc


def kernel(L0_attn_x, L0_ffn_w_gate, L0_ffn_w_up, L0_ffn_w_down):
    x = np.asarray(L0_attn_x, dtype=np.float32)
    wg = np.asarray(L0_ffn_w_gate, dtype=np.int32)
    wu = np.asarray(L0_ffn_w_up, dtype=np.int32)
    wd = np.asarray(L0_ffn_w_down, dtype=np.int32)
    wgT = np.ascontiguousarray(
        wg.reshape(HC, 128, DC, 128).transpose(0, 3, 2, 1).reshape(HC * 128, DC * 128))
    wuT = np.ascontiguousarray(
        wu.reshape(HC, 128, DC, 128).transpose(0, 3, 2, 1).reshape(HC * 128, DC * 128))
    wdT = np.ascontiguousarray(
        wd.reshape(DC, 128, HC, 128).transpose(0, 3, 2, 1).reshape(DC * 128, HC * 128))
    ident = np.eye(128, dtype=ml_dtypes.bfloat16)
    nc = build_nc()
    if not nc.is_finalized():
        nc.finalize()
    in_maps = [dict(x=np.ascontiguousarray(x[c]), wgT=wgT, wuT=wuT, wdT=wdT, ident=ident)
               for c in range(8)]
    res = run_bass_kernel_spmd(nc, in_maps, core_ids=list(range(8)))
    return np.stack([np.asarray(res.results[c]["y"]) for c in range(8)], axis=0)
